# version 08
# baseline (speedup 1.0000x reference)
import numpy as np
import concourse.bass as bass
import concourse.mybir as mybir
from concourse.bass_utils import run_bass_kernel_spmd
from contextlib import ExitStack

F32 = mybir.dt.float32
BF16 = mybir.dt.bfloat16
I16 = mybir.dt.int16
I8 = mybir.dt.int8
AF = mybir.ActivationFunctionType
ALU = mybir.AluOpType
AX = mybir.AxisListType

T = 2048
D = 1024
NH = 16
NTB = 16
NFC = 8
NT4 = 4

ENGINES = ("sync", "tensor", "vector", "scalar", "gpsimd")
FFN_TILES = NTB


class Op:
    __slots__ = ("idx", "eng", "fn", "reads", "writes", "sig", "sem",
                 "deps", "sig_val", "small")

    def __init__(self, idx, eng, fn, reads, writes, sig, sem, small):
        self.idx, self.eng, self.fn = idx, eng, fn
        self.reads, self.writes = tuple(reads), tuple(writes)
        self.sig, self.sem = sig, sem
        self.deps, self.sig_val = [], None
        self.small = small


class Prog:
    SERIAL = False

    def __init__(self):
        self.ops = []
        self.last_writer = {}
        self.readers = {}

    def op(self, eng, fn, reads=(), writes=(), dma_sem=None, small=False):
        idx = len(self.ops)
        o = Op(idx, eng, fn, reads, writes, 16 if dma_sem else 0,
               dma_sem or eng, small)
        deps = set()
        if Prog.SERIAL and idx > 0:
            deps.add(idx - 1)
        for b in o.reads:
            w = self.last_writer.get(b)
            if w is not None:
                deps.add(w)
        for b in o.writes:
            w = self.last_writer.get(b)
            if w is not None:
                deps.add(w)
            deps.update(self.readers.get(b, ()))
        deps.discard(idx)
        o.deps = sorted(deps)
        self.ops.append(o)
        for b in o.reads:
            self.readers.setdefault(b, []).append(idx)
        for b in o.writes:
            self.last_writer[b] = idx
            self.readers[b] = []
        return idx

    def finalize(self):
        ops = self.ops
        needed = set()
        for o in ops:
            for d in o.deps:
                if ops[d].eng != o.eng or ops[d].small:
                    needed.add(d)
        ctr = {}
        for o in ops:
            if o.sig == 16:
                ctr[o.sem] = ctr.get(o.sem, 0) + 16
                o.sig_val = ctr[o.sem]
            elif o.idx in needed:
                o.sig = 1
                ctr[o.sem] = ctr.get(o.sem, 0) + 1
                o.sig_val = ctr[o.sem]
        per_eng = {e: [] for e in ENGINES}
        waited = {e: {} for e in ENGINES}
        for o in ops:
            waits = {}
            for d in o.deps:
                od = ops[d]
                if od.eng == o.eng and not od.small:
                    continue
                sem = od.sem
                if waits.get(sem, 0) < od.sig_val:
                    waits[sem] = od.sig_val
            w2 = []
            for sem, val in waits.items():
                if waited[o.eng].get(sem, 0) < val:
                    waited[o.eng][sem] = val
                    w2.append((sem, val))
            per_eng[o.eng].append((o.fn, w2, o.sem if o.sig else None, o.sig))
        return per_eng, ctr


def _gran(off, n, tag):
    return [(tag, g) for g in range(off // 512, (off + n + 511) // 512)]


def build_nc(stages=3):
    nc = bass.Bass("TRN2", target_bir_lowering=False)
    x_in = nc.dram_tensor("x", [T, D], F32, kind="ExternalInput").ap()
    wq_in = nc.dram_tensor("wq", [D, 2048], F32, kind="ExternalInput").ap()
    wk_in = nc.dram_tensor("wk", [D, 2048], F32, kind="ExternalInput").ap()
    wv_in = nc.dram_tensor("wv", [D, 2048], F32, kind="ExternalInput").ap()
    wo_in = nc.dram_tensor("wo", [2048, D], F32, kind="ExternalInput").ap()
    fw_in = nc.dram_tensor("fw", [D, D], I8, kind="ExternalInput").ap()
    fs_in = nc.dram_tensor("fs", [D], F32, kind="ExternalInput").ap()
    fb_in = nc.dram_tensor("fb", [D], F32, kind="ExternalInput").ap()
    out_t = nc.dram_tensor("out", [T, D], F32, kind="ExternalOutput").ap()

    es = ExitStack()
    sb = lambda n, shp, dt: es.enter_context(nc.sbuf_tensor(n, shp, dt))

    resid = sb("resid", [128, NTB * D], F32)
    pool = sb("pool", [128, 7424], F32)
    bpool = sb("bpool", [128, 55328], BF16)
    i8p = sb("i8p", [128, 1024], I8)
    ci = sb("ci", [128, D], I16)
    stats = sb("stats", [128, 16], F32)
    invz = sb("invz", [128, 4], F32)
    onesq = sb("onesq", [128, 128], F32)
    ident = sb("ident", [128, 128], F32)
    onesb = sb("onesb", [128, 128], BF16)
    identb = sb("identb", [128, 128], BF16)

    O_ZG = (0, 1024)
    O_RD = (2048, 3072)
    O_CIF = (4096, 4096)
    O_HB, O_WS = 5120, 6144
    O_WSROW, O_FBROW = 0, 1024
    O_ST2 = 7168
    SUMS, SSQS, MEANS, SVEC, MXS, MNS, XSCS, IXSS, GS, XSHS = range(10)
    O_OT, N_OT = 0, 512
    O_VF, N_VF = 512, 2048
    O_PR, N_PR = 4608, 512
    O_WSTG = 5120
    O_ZT2, O_SQ2 = 3584, 2560
    POOL_COLS = 7424

    def pap(off, n):
        return pool[:, off:off + n]

    def pk(off, n):
        return _gran(off, n, "P")

    B_WT = 0
    B_XQTM, B_XQFM = 8192, 9216
    B_Y2H, B_Y2L = 0, 16384
    B_WP = 32768
    B_FWBF = 8192
    B_KHH, B_KHL = 36864, 38912
    B_QHH, B_QHL = 40960, 43008
    B_VAH, B_VAL = 45056, 47120
    B_PRH, B_PRL = 49184, 50208
    B_OFH, B_OFL = 51232, 53280
    BPOOL_ELS = 55328

    def bap(off, n):
        return bpool[:, off:off + n]

    def bk(off, n):
        return _gran(off, n, "B")

    ps_a = es.enter_context(nc.psum_tensor("ps_a", [128, 512], F32))
    ps_b = es.enter_context(nc.psum_tensor("ps_b", [128, 512], F32))
    ps_c = es.enter_context(nc.psum_tensor("ps_c", [128, 512], F32))
    ps_d = es.enter_context(nc.psum_tensor("ps_d", [128, 512], F32))
    ps_o = es.enter_context(nc.psum_tensor("ps_o", [128, 1024], F32))
    ps_e = es.enter_context(nc.psum_tensor("ps_e", [128, 512], F32))
    ps_tb = es.enter_context(nc.psum_tensor("ps_tb", [128, 512], BF16))

    semnames = list(ENGINES) + ["din", "dinx", "dinw", "dinr1", "dinr2", "dout"]
    sems = {n: es.enter_context(nc.semaphore("s_" + n)) for n in semnames}

    P = Prog()
    mmA = [(ps_a, "psA"), (ps_b, "psB")]
    mmT = [(ps_c, "psC"), (ps_d, "psD")]

    def col(c):
        return stats[:, c:c + 1]

    C_MEAN, C_S, C_SUM, C_SSQ, C_VAR, C_SD, C_MSQ, C_AMAX, C_XSC, C_IXS, \
        C_XSH = range(11)

    def rap(tb, c0=0, c1=D):
        return resid[:, tb * D + c0: tb * D + c1]

    def rk(tb):
        return ("R", tb)

    P.op("sync", lambda e: e.dma_start(
        out=resid[:].rearrange("p (tt f) -> p tt f", tt=NTB),
        in_=x_in.rearrange("(tt p) f -> p tt f", p=128)),
        writes=[rk(tb) for tb in range(NTB)], dma_sem="dinx")

    P.op("vector", lambda e: e.memset(onesq[:], 1.0), writes=["onesq"])
    P.op("vector", lambda e: e.memset(onesb[:], 1.0), writes=["onesb"])
    P.op("gpsimd", lambda e: e.affine_select(
        out=ident[:], in_=onesq[:], compare_op=ALU.is_equal,
        pattern=[[-1, 128]], base=0, channel_multiplier=1, fill=0.0),
        reads=["onesq"], writes=["ident"])
    P.op("gpsimd", lambda e: e.affine_select(
        out=identb[:], in_=onesb[:], compare_op=ALU.is_equal,
        pattern=[[-1, 128]], base=0, channel_multiplier=1, fill=0.0),
        reads=["onesb"], writes=["identb"])

    def build_wt2():
        for oc in range(NFC):
            def mkd(oc=oc):
                def f(e):
                    return e.dma_start(out=i8p[:],
                                       in_=fw_in[oc * 128:(oc + 1) * 128, :])
                return f
            P.op("sync", mkd(), writes=["i8p"], dma_sem="din")
            P.op("vector", lambda e: e.tensor_copy(bap(B_FWBF, 1024), i8p[:]),
                 reads=["i8p"], writes=bk(B_FWBF, 1024))
            for g in range(2):
                def mktr(g=g):
                    def f(e):
                        last = None
                        for j in range(4):
                            icc = g * 4 + j
                            last = e.matmul(
                                ps_tb[:, j * 128:(j + 1) * 128],
                                bap(B_FWBF + icc * 128, 128), identb[:],
                                is_transpose=True, start=(j == 0), stop=(j == 3))
                        return last
                    return f
                P.op("tensor", mktr(), reads=bk(B_FWBF, 1024) + ["identb"],
                     writes=["psTB"])

                def mkev(oc=oc, g=g):
                    def f(e):
                        return e.activation(
                            out=bpool[:, B_WT:B_WT + 8192]
                            .rearrange("p (ic o) -> p ic o", ic=NFC)
                            [:, g * 4:(g + 1) * 4, oc * 128:(oc + 1) * 128],
                            in_=ps_tb[:].rearrange("p (a b) -> p a b", a=4),
                            func=AF.Copy)
                    return f
                wks = []
                for icc in range(g * 4, g * 4 + 4):
                    wks += _gran(B_WT + icc * 1024 + oc * 128, 128, "B")
                P.op("scalar", mkev(), reads=["psTB"], writes=wks)

    WTB_ALL = bk(B_WT, 8192)
    def ln_stats(src_ap, src_keys, o_sq):
        sq_ap = pap(o_sq, D)
        sq_k = pk(o_sq, D)
        P.op("vector", lambda e: e.tensor_reduce(
            out=col(C_SUM), in_=src_ap, op=ALU.add, axis=AX.X),
            reads=src_keys, writes=[("st", C_SUM)], small=True)
        P.op("scalar", lambda e: e.activation(out=sq_ap, in_=src_ap,
                                              func=AF.Square),
            reads=src_keys, writes=sq_k)
        P.op("vector", lambda e: e.tensor_reduce(
            out=col(C_SSQ), in_=sq_ap, op=ALU.add, axis=AX.X),
            reads=sq_k, writes=[("st", C_SSQ)], small=True)
        P.op("vector", lambda e: e.tensor_scalar(
            out=col(C_MEAN), in0=col(C_SUM), scalar1=1.0 / D, scalar2=None,
            op0=ALU.mult), reads=[("st", C_SUM)], writes=[("st", C_MEAN)],
            small=True)
        P.op("vector", lambda e: e.tensor_scalar(
            out=col(C_MSQ), in0=col(C_MEAN), scalar1=col(C_MEAN),
            scalar2=-1e-5, op0=ALU.mult, op1=ALU.add),
            reads=[("st", C_MEAN)], writes=[("st", C_MSQ)], small=True)
        P.op("vector", lambda e: e.scalar_tensor_tensor(
            out=col(C_VAR), in0=col(C_SSQ), scalar=1.0 / D, in1=col(C_MSQ),
            op0=ALU.mult, op1=ALU.subtract),
            reads=[("st", C_SSQ), ("st", C_MSQ)], writes=[("st", C_VAR)],
            small=True)
        P.op("scalar", lambda e: e.activation(
            out=col(C_SD), in_=col(C_VAR), func=AF.Sqrt),
            reads=[("st", C_VAR)], writes=[("st", C_SD)], small=True)
        P.op("vector", lambda e: e.reciprocal(col(C_S), col(C_SD)),
             reads=[("st", C_SD)], writes=[("st", C_S)], small=True)

    def bcast_rows(src_off, dst_off, scale):
        for half in range(2):
            def mkmm(half=half):
                def f(e):
                    last = None
                    for j in range(4):
                        r = half * 4 + j
                        last = e.matmul(ps_e[:, j * 128:(j + 1) * 128],
                                        onesq[0:1, :],
                                        pool[0:1, src_off + r * 128:
                                             src_off + (r + 1) * 128],
                                        start=(j == 0), stop=(j == 3))
                    return last
                return f
            P.op("tensor", mkmm(), reads=pk(src_off, D) + ["onesq"],
                 writes=["psE"])

            def mkev(half=half):
                def f(e):
                    return e.activation(
                        out=pap(dst_off + half * 512, 512), in_=ps_e[:],
                        func=AF.Copy, scale=scale)
                return f
            P.op("scalar", mkev(), reads=["psE"],
                 writes=pk(dst_off + half * 512, 512))

    ffn_calls = [0]

    def st2(c, t=None):
        base = O_ST2 + c * 16
        if t is None:
            return pool[:, base:base + 16]
        return pool[:, base + t:base + t + 1]

    STK = pk(O_ST2, 11 * 16)
    TMPS = 10

    def ffn_stage():
        ffn_calls[0] += 1
        this_call = ffn_calls[0]
        P.op("sync", lambda e: e.dma_start(
            out=pool[0:1, O_WSROW:O_WSROW + D],
            in_=fs_in.rearrange("(o d) -> o d", o=1)),
            writes=pk(O_WSROW, D), dma_sem="dinr1")
        P.op("sync", lambda e: e.dma_start(
            out=pool[0:1, O_FBROW:O_FBROW + D],
            in_=fb_in.rearrange("(o d) -> o d", o=1)),
            writes=pk(O_FBROW, D), dma_sem="dinr2")
        bcast_rows(O_WSROW, O_WS, 1.0)
        bcast_rows(O_FBROW, O_HB, 0.5)
        build_wt2()
        hb, hbk = pap(O_HB, D), pk(O_HB, D)
        wsk = pk(O_WS, D)

        for tb in range(NTB):
            srcap = rap(tb)
            zgd, zgdk = pap(O_ZG[tb % 2], D), pk(O_ZG[tb % 2], D)
            P.op("vector", lambda e, s=srcap, t=tb, zgd=zgd: e.tensor_scalar(
                out=zgd, in0=s, scalar1=1.0, scalar2=None, op0=ALU.mult,
                op1=ALU.add, accum_out=st2(SUMS, t)),
                reads=[rk(tb)], writes=zgdk + STK, small=True)
            P.op("scalar", lambda e, s=srcap, zgd=zgd: e.activation(
                out=zgd, in_=s, func=AF.Square),
                reads=[rk(tb)], writes=zgdk)
            P.op("vector", lambda e, zgd=zgd, t=tb: e.tensor_scalar(
                out=zgd, in0=zgd, scalar1=1.0, scalar2=None,
                op0=ALU.mult, op1=ALU.add, accum_out=st2(SSQS, t)),
                reads=zgdk, writes=zgdk + STK, small=True)

        P.op("vector", lambda e: e.tensor_scalar(
            out=st2(MEANS), in0=st2(SUMS), scalar1=1.0 / D, scalar2=None,
            op0=ALU.mult), reads=STK, writes=STK, small=True)
        P.op("vector", lambda e: e.tensor_mul(
            st2(TMPS), st2(MEANS), st2(MEANS)), reads=STK, writes=STK,
            small=True)
        P.op("vector", lambda e: e.tensor_scalar(
            out=st2(TMPS), in0=st2(TMPS), scalar1=-1e-5, scalar2=None,
            op0=ALU.add), reads=STK, writes=STK, small=True)
        P.op("vector", lambda e: e.scalar_tensor_tensor(
            out=st2(TMPS), in0=st2(SSQS), scalar=1.0 / D, in1=st2(TMPS),
            op0=ALU.mult, op1=ALU.subtract), reads=STK, writes=STK, small=True)
        P.op("scalar", lambda e: e.activation(
            out=st2(TMPS), in_=st2(TMPS), func=AF.Sqrt),
            reads=STK, writes=STK, small=True)
        P.op("vector", lambda e: e.reciprocal(st2(SVEC), st2(TMPS)),
             reads=STK, writes=STK, small=True)

        for tb in range(NTB):
            srcap = rap(tb)
            zgd, zgdk = pap(O_ZG[tb % 2], D), pk(O_ZG[tb % 2], D)
            P.op("vector", lambda e, s=srcap, t=tb, zgd=zgd: e.tensor_scalar(
                out=zgd, in0=s, scalar1=st2(MEANS, t), scalar2=None,
                op0=ALU.subtract, op1=ALU.max, accum_out=st2(MXS, t)),
                reads=[rk(tb)] + STK, writes=zgdk + STK, small=True)
            P.op("vector", lambda e, s=srcap, t=tb, zgd=zgd: e.tensor_scalar(
                out=zgd, in0=s, scalar1=st2(MEANS, t), scalar2=None,
                op0=ALU.subtract, op1=ALU.min, accum_out=st2(MNS, t)),
                reads=[rk(tb)] + STK, writes=zgdk + STK, small=True)

        P.op("vector", lambda e: e.tensor_scalar(
            out=st2(TMPS), in0=st2(MNS), scalar1=-1.0, scalar2=None,
            op0=ALU.mult), reads=STK, writes=STK, small=True)
        P.op("vector", lambda e: e.tensor_max(
            st2(TMPS), st2(TMPS), st2(MXS)), reads=STK, writes=STK, small=True)
        P.op("vector", lambda e: e.tensor_mul(
            st2(TMPS), st2(TMPS), st2(SVEC)), reads=STK, writes=STK, small=True)
        P.op("vector", lambda e: e.tensor_scalar(
            out=st2(XSCS), in0=st2(TMPS), scalar1=1e-6, scalar2=1.0 / 127.0,
            op0=ALU.max, op1=ALU.mult), reads=STK, writes=STK, small=True)
        P.op("vector", lambda e: e.reciprocal(st2(IXSS), st2(XSCS)),
             reads=STK, writes=STK, small=True)
        P.op("vector", lambda e: e.tensor_mul(
            st2(GS), st2(SVEC), st2(IXSS)), reads=STK, writes=STK, small=True)
        P.op("vector", lambda e: e.tensor_scalar(
            out=st2(XSHS), in0=st2(XSCS), scalar1=0.5, scalar2=None,
            op0=ALU.mult), reads=STK, writes=STK, small=True)

        for tb in range(NTB):
            srcap = rap(tb)
            p_ = tb % 2
            zg, zgk = pap(O_ZG[p_], D), pk(O_ZG[p_], D)
            rd, rdk = pap(O_RD[p_], D), pk(O_RD[p_], D)
            cf, cfk = pap(O_CIF[p_], D), pk(O_CIF[p_], D)
            ci_ap = ci[:, 0:D]
            cik = ("ci", 0)
            P.op("vector", lambda e, s=srcap, t=tb, zg=zg: e.tensor_scalar(
                out=zg, in0=s, scalar1=st2(MEANS, t), scalar2=st2(GS, t),
                op0=ALU.subtract, op1=ALU.mult),
                reads=[rk(tb)] + STK, writes=zgk)
            P.op("scalar", lambda e, zg=zg, ci_ap=ci_ap: e.activation(
                out=ci_ap, in_=zg, func=AF.Copy, bias=128.5),
                reads=zgk, writes=[cik])
            P.op("scalar", lambda e, cf=cf, ci_ap=ci_ap: e.activation(
                out=cf, in_=ci_ap, func=AF.Copy), reads=[cik], writes=cfk)
            P.op("vector", lambda e, zg=zg, rd=rd, cf=cf: e.scalar_tensor_tensor(
                out=rd, in0=zg, scalar=128.5, in1=cf,
                op0=ALU.add, op1=ALU.subtract), reads=zgk + cfk, writes=rdk)
            P.op("vector", lambda e, rd=rd: e.tensor_scalar(
                out=rd, in0=rd, scalar1=0.0, scalar2=None, op0=ALU.is_ge),
                reads=rdk, writes=rdk)
            P.op("vector", lambda e, rd=rd, cf=cf, zg=zg: e.scalar_tensor_tensor(
                out=zg, in0=rd, scalar=1.0, in1=cf, op0=ALU.mult, op1=ALU.add),
                reads=rdk + cfk, writes=zgk)
            P.op("scalar", lambda e, zg=zg: e.activation(
                out=bap(B_XQTM, 1024), in_=zg, func=AF.Copy, bias=-129.0),
                reads=zgk, writes=bk(B_XQTM, 1024))
            P.op("gpsimd", lambda e, s=srcap: e.tensor_add(s, s, hb),
                 reads=[rk(tb)] + hbk, writes=[rk(tb)])
            for g in range(2):
                def mktr(g=g):
                    def f(e):
                        last = None
                        for j in range(4):
                            icc = g * 4 + j
                            last = e.matmul(
                                ps_tb[:, j * 128:(j + 1) * 128],
                                bap(B_XQTM + icc * 128, 128), identb[:],
                                is_transpose=True, start=(j == 0), stop=(j == 3))
                        return last
                    return f
                P.op("tensor", mktr(), reads=bk(B_XQTM, 1024) + ["identb"],
                     writes=["psTB"])
                P.op("scalar", lambda e, g=g: e.activation(
                    out=bap(B_XQFM + g * 512, 512), in_=ps_tb[:],
                    func=AF.Copy), reads=["psTB"],
                    writes=bk(B_XQFM + g * 512, 512))
            for oh in range(2):
                bank, bkey = mmA[oh]

                def mkmm(oh=oh, bank=bank):
                    def f(e):
                        last = None
                        for icc in range(NFC):
                            last = e.matmul(
                                bank[:],
                                bap(B_XQFM + icc * 128, 128),
                                bpool[:, B_WT + icc * D + oh * 512:
                                      B_WT + icc * D + (oh + 1) * 512],
                                start=(icc == 0), stop=(icc == NFC - 1))
                        return last
                    return f
                P.op("tensor", mkmm(),
                     reads=WTB_ALL + bk(B_XQFM, 1024), writes=[bkey])
                uo = O_RD[tb % 2] + oh * 512
                P.op("vector", lambda e, oh=oh, bank=bank, uo=uo: e.tensor_mul(
                    pap(uo, 512), bank[:],
                    pool[:, O_WS + oh * 512:O_WS + (oh + 1) * 512]),
                    reads=[bkey] + wsk, writes=pk(uo, 512))
                P.op("vector", lambda e, oh=oh, s=srcap, t=tb, uo=uo:
                     e.scalar_tensor_tensor(
                         out=s[:, oh * 512:(oh + 1) * 512], in0=pap(uo, 512),
                         scalar=st2(XSHS, t), in1=s[:, oh * 512:(oh + 1) * 512],
                         op0=ALU.mult, op1=ALU.add),
                     reads=pk(uo, 512) + STK + [rk(tb)], writes=[rk(tb)])

    if stages >= 1:
        ffn_stage()

    if stages >= 2:
        for tb in range(NTB):
            src = rap(tb)
            ln_stats(src, [rk(tb)], O_SQ2)
            z2, z2k = pap(O_ZT2, D), pk(O_ZT2, D)
            P.op("vector", lambda e, s=src: e.tensor_scalar(
                out=z2, in0=s, scalar1=col(C_MEAN), scalar2=col(C_S),
                op0=ALU.subtract, op1=ALU.mult),
                reads=[rk(tb), ("st", C_MEAN), ("st", C_S)], writes=z2k)
            for g in range(2):
                bank, bkey = mmT[g]

                def mktr(g=g, z2=z2, bank=bank):
                    def f(e):
                        last = None
                        for j in range(4):
                            fc = g * 4 + j
                            last = e.matmul(
                                bank[:, j * 128:(j + 1) * 128],
                                z2[:, fc * 128:(fc + 1) * 128], ident[:],
                                is_transpose=True, start=(j == 0), stop=(j == 3))
                        return last
                    return f
                P.op("tensor", mktr(), reads=z2k + ["ident"], writes=[bkey])
                def mkev(g=g, tb=tb, bank=bank):
                    def f(e):
                        return e.activation(
                            out=bpool[:, B_Y2H:B_Y2H + 16384]
                            .rearrange("p (fc t) -> p fc t", fc=NFC)
                            [:, g * 4:(g + 1) * 4, tb * 128:(tb + 1) * 128],
                            in_=bank[:].rearrange("p (a b) -> p a b", a=4),
                            func=AF.Copy)
                    return f
                y2ks = []
                for fc in range(g * 4, g * 4 + 4):
                    y2ks += _gran(B_Y2H + fc * T + tb * 128, 128, "B")
                P.op("scalar", mkev(), reads=[bkey], writes=y2ks)

                def mklo(g=g, tb=tb, bank=bank):
                    def f(e):
                        return e.tensor_sub(
                            bpool[:, B_Y2L:B_Y2L + 16384]
                            .rearrange("p (fc t) -> p fc t", fc=NFC)
                            [:, g * 4:(g + 1) * 4, tb * 128:(tb + 1) * 128],
                            bank[:].rearrange("p (a b) -> p a b", a=4),
                            bpool[:, B_Y2H:B_Y2H + 16384]
                            .rearrange("p (fc t) -> p fc t", fc=NFC)
                            [:, g * 4:(g + 1) * 4, tb * 128:(tb + 1) * 128])
                    return f
                y2kl = []
                for fc in range(g * 4, g * 4 + 4):
                    y2kl += _gran(B_Y2L + fc * T + tb * 128, 128, "B")
                P.op("vector", mklo(), reads=[bkey] + y2ks, writes=y2kl)
        Y2_ALL = bk(B_Y2H, 16384) + bk(B_Y2L, 16384)

        wdma_ctr = [0]

        def wdma(h, p):
            buf = wdma_ctr[0] % 2
            wdma_ctr[0] += 1
            hi_off = B_WP + buf * 2048
            lo_off = hi_off + 1024
            if p < 3:
                w_in = (wq_in, wk_in, wv_in)[p]

                def f32(e, w_in=w_in, h=h):
                    return e.dma_start(
                        out=pap(O_WSTG, 1024).rearrange("p (fc o) -> p fc o", fc=NFC),
                        in_=w_in.rearrange("(fc p) o -> p fc o", p=128)
                        [:, :, h * 128:(h + 1) * 128])
            else:
                def f32(e, h=h):
                    return e.dma_start(out=pap(O_WSTG, 1024),
                                       in_=wo_in[h * 128:(h + 1) * 128, :])
            P.op("sync", f32, writes=pk(O_WSTG, 1024), dma_sem="dinw")
            P.op("scalar", lambda e, hi_off=hi_off: e.activation(
                out=bap(hi_off, 1024), in_=pap(O_WSTG, 1024), func=AF.Copy),
                reads=pk(O_WSTG, 1024), writes=bk(hi_off, 1024))
            P.op("vector", lambda e, hi_off=hi_off, lo_off=lo_off: e.tensor_sub(
                bap(lo_off, 1024), pap(O_WSTG, 1024), bap(hi_off, 1024)),
                reads=pk(O_WSTG, 1024) + bk(hi_off, 1024),
                writes=bk(lo_off, 1024))
            return (bap(hi_off, 1024), bap(lo_off, 1024),
                    bk(hi_off, 1024) + bk(lo_off, 1024))

        pending = {}
        pending[(0, 0)] = wdma(0, 0)
        pending[(0, 1)] = wdma(0, 1)

        def get_w(h, p):
            if (h, p) not in pending:
                pending[(h, p)] = wdma(h, p)
            np_ = (h, p + 1) if p < 3 else (h + 1, 0)
            if np_[0] < NH and np_ not in pending:
                pending[np_] = wdma(*np_)
            return pending.pop((h, p))

        for h in range(NH):
            def proj(h, p, dst_off, pair=None):
                w_hi, w_lo, w_keys = get_w(h, p)
                for t4 in range(NT4):
                    bank, bkey = mmA[t4 % 2]

                    def mkmm(t4=t4, bank=bank, w_hi=w_hi, w_lo=w_lo):
                        def f(e):
                            last = None
                            n = 3 * NFC
                            i = 0
                            for fc in range(NFC):
                                for (wap, yoff) in ((w_hi, B_Y2H), (w_lo, B_Y2H),
                                                    (w_hi, B_Y2L)):
                                    last = e.matmul(
                                        bank[:],
                                        wap[:, fc * 128:(fc + 1) * 128],
                                        bpool[:, yoff + fc * T + t4 * 512:
                                              yoff + fc * T + (t4 + 1) * 512],
                                        start=(i == 0), stop=(i == n - 1))
                                    i += 1
                            return last
                        return f
                    P.op("tensor", mkmm(), reads=Y2_ALL + w_keys, writes=[bkey])
                    if pair is None:
                        def mkev(t4=t4, bank=bank, dst_off=dst_off):
                            def f(e):
                                return e.activation(
                                    out=pap(dst_off + t4 * 512, 512),
                                    in_=bank[:], func=AF.Copy)
                            return f
                        P.op("scalar", mkev(), reads=[bkey],
                             writes=pk(dst_off + t4 * 512, 512))
                    else:
                        bh, bl = pair
                        def mkhi(t4=t4, bank=bank, bh=bh):
                            def f(e):
                                return e.activation(
                                    out=bap(bh + t4 * 512, 512),
                                    in_=bank[:], func=AF.Copy)
                            return f
                        P.op("scalar", mkhi(), reads=[bkey],
                             writes=bk(bh + t4 * 512, 512))
                        def mklo(t4=t4, bank=bank, bh=bh, bl=bl):
                            def f(e):
                                return e.tensor_sub(
                                    bap(bl + t4 * 512, 512), bank[:],
                                    bap(bh + t4 * 512, 512))
                            return f
                        P.op("vector", mklo(),
                             reads=[bkey] + bk(bh + t4 * 512, 512),
                             writes=bk(bl + t4 * 512, 512))

            proj(h, 0, None, pair=(B_QHH, B_QHL))
            proj(h, 1, None, pair=(B_KHH, B_KHL))
            proj(h, 2, O_VF, None)
            QH_ALL = bk(B_QHH, 2048) + bk(B_QHL, 2048)
            KH_ALL = bk(B_KHH, 2048) + bk(B_KHL, 2048)
            VF_ALL = pk(O_VF, N_VF)

            for g in range(4):
                bank, bkey = mmT[0]

                def mktr(g=g, bank=bank):
                    def f(e):
                        last = None
                        for j in range(4):
                            kb = g * 4 + j
                            last = e.matmul(
                                bank[:, j * 128:(j + 1) * 128],
                                pap(O_VF + kb * 128, 128), ident[:],
                                is_transpose=True, start=(j == 0), stop=(j == 3))
                        return last
                    return f
                P.op("tensor", mktr(), reads=VF_ALL + ["ident"],
                     writes=[bkey])

                def mkev(g=g, bank=bank):
                    def f(e):
                        return e.activation(
                            out=bap(B_VAH, 2064)
                            .rearrange("p (b c) -> p b c", c=129)
                            [:, g * 4:(g + 1) * 4, 0:128],
                            in_=bank[:].rearrange("p (a b) -> p a b", a=4),
                            func=AF.Copy)
                    return f
                P.op("scalar", mkev(), reads=[bkey],
                     writes=bk(B_VAH + g * 4 * 129, 4 * 129))

                def mklo(g=g, bank=bank):
                    def f(e):
                        return e.tensor_sub(
                            bap(B_VAL, 2064)
                            .rearrange("p (b c) -> p b c", c=129)
                            [:, g * 4:(g + 1) * 4, 0:128],
                            bank[:].rearrange("p (a b) -> p a b", a=4),
                            bap(B_VAH, 2064)
                            .rearrange("p (b c) -> p b c", c=129)
                            [:, g * 4:(g + 1) * 4, 0:128])
                    return f
                P.op("vector", mklo(),
                     reads=[bkey] + bk(B_VAH + g * 4 * 129, 4 * 129),
                     writes=bk(B_VAL + g * 4 * 129, 4 * 129))
            P.op("vector", lambda e: e.memset(
                bap(B_VAH, 2064).rearrange("p (b c) -> p b c", c=129)
                [:, :, 128:129], 1.0), writes=bk(B_VAH, 2064))
            P.op("vector", lambda e: e.memset(
                bap(B_VAL, 2064).rearrange("p (b c) -> p b c", c=129)
                [:, :, 128:129], 0.0), writes=bk(B_VAL, 2064))
            VA_ALL = bk(B_VAH, 2064) + bk(B_VAL, 2064)

            for t4 in range(NT4):
                n_off = 4 * t4
                nkc = n_off + 4
                bank_first = {0: True, 1: True}
                bank_total = {0: 0, 1: 0}
                for kc in range(nkc):
                    jmin = 0 if kc < n_off else kc - n_off
                    for j in range(jmin, 4):
                        bank_total[j // 2] += 3
                bank_done = {0: 0, 1: 0}
                for kc in range(nkc):
                    diag = kc >= n_off
                    bank, bkey = mmA[kc % 2]
                    pslot = kc % 2
                    prh_ap = bap(B_PRH + pslot * 512, 512)
                    prl_ap = bap(B_PRL + pslot * 512, 512)
                    prhk = bk(B_PRH + pslot * 512, 512)
                    prlk = bk(B_PRL + pslot * 512, 512)
                    jmin = 0 if not diag else kc - n_off

                    PASSES = ((B_KHH, B_QHH), (B_KHL, B_QHH), (B_KHH, B_QHL))
                    if not diag:
                        def mksc(kc=kc, bank=bank, t4=t4):
                            def f(e):
                                last = None
                                for i, (kb, qb) in enumerate(PASSES):
                                    last = e.matmul(
                                        bank[:],
                                        bap(kb + kc * 128, 128),
                                        bap(qb + t4 * 512, 512),
                                        start=(i == 0), stop=(i == 2))
                                return last
                            return f
                    else:
                        def mksc(kc=kc, bank=bank, t4=t4, jmin=jmin):
                            def f(e):
                                last = None
                                n = (4 - jmin) * 3
                                i = 0
                                for j in range(jmin, 4):
                                    for (kb, qb) in PASSES:
                                        last = e.matmul(
                                            bank[:, j * 128:(j + 1) * 128],
                                            bap(kb + kc * 128, 128),
                                            bap(qb + t4 * 512 + j * 128, 128),
                                            start=(i == 0), stop=(i == n - 1))
                                        i += 1
                                return last
                            return f
                    P.op("tensor", mksc(), reads=QH_ALL + KH_ALL,
                         writes=[bkey])

                    if not diag:
                        pbank, pbkey = (ps_e, "psE") if kc % 2 == 0 else (ps_d, "psD")
                        P.op("scalar", lambda e, bank=bank, pbank=pbank:
                             e.activation(out=pbank[:], in_=bank[:],
                                          func=AF.Exp, scale=0.125),
                             reads=[bkey], writes=[pbkey])
                        P.op("vector", lambda e, prh_ap=prh_ap, pbank=pbank:
                             e.tensor_copy(prh_ap, pbank[:]),
                             reads=[pbkey], writes=prhk, small=True)
                        P.op("vector", lambda e, prh_ap=prh_ap, prl_ap=prl_ap,
                             pbank=pbank: e.tensor_sub(prl_ap, pbank[:], prh_ap),
                             reads=[pbkey] + prhk, writes=prlk)
                    else:
                        p32 = pap(O_PR, 512)
                        p32k = pk(O_PR, 512)

                        def mkexp(bank=bank, jmin=jmin, p32=p32):
                            def f(e):
                                return e.activation(
                                    out=p32[:, jmin * 128:512],
                                    in_=bank[:, jmin * 128:512],
                                    func=AF.Exp, scale=0.125)
                            return f
                        P.op("scalar", mkexp(), reads=[bkey], writes=p32k)

                        def mkmask(jmin=jmin, p32=p32):
                            def f(e):
                                return e.affine_select(
                                    out=p32[:, jmin * 128:(jmin + 1) * 128],
                                    in_=p32[:, jmin * 128:(jmin + 1) * 128],
                                    compare_op=ALU.is_ge, pattern=[[1, 128]],
                                    base=0, channel_multiplier=-1, fill=0.0)
                            return f
                        P.op("gpsimd", mkmask(), reads=p32k, writes=p32k)
                        P.op("vector", lambda e, prh_ap=prh_ap, jmin=jmin,
                             p32=p32: e.tensor_copy(
                                 prh_ap[:, jmin * 128:512],
                                 p32[:, jmin * 128:512]),
                             reads=p32k, writes=prhk, small=True)
                        P.op("vector", lambda e, prh_ap=prh_ap, prl_ap=prl_ap,
                             jmin=jmin, p32=p32: e.tensor_sub(
                                 prl_ap[:, jmin * 128:512],
                                 p32[:, jmin * 128:512],
                                 prh_ap[:, jmin * 128:512]),
                             reads=p32k + prhk, writes=prlk)

                    for bhalf in (0, 1):
                        js = [j for j in range(jmin, 4) if j // 2 == bhalf]
                        if not js:
                            continue
                        starts = bank_first[bhalf]
                        bank_first[bhalf] = False
                        bank_done[bhalf] += 3 * len(js)
                        stops = bank_done[bhalf] == bank_total[bhalf]

                        def mkpv(kc=kc, js=tuple(js), starts=starts,
                                 stops=stops, prh_ap=prh_ap, prl_ap=prl_ap):
                            def f(e):
                                last = None
                                n = 3 * len(js)
                                i = 0
                                for j in js:
                                    for (pa, vb) in ((prh_ap, B_VAH),
                                                     (prl_ap, B_VAH),
                                                     (prh_ap, B_VAL)):
                                        last = e.matmul(
                                            ps_o[:, j * 256:j * 256 + 129],
                                            pa[:, j * 128:(j + 1) * 128],
                                            bap(vb + kc * 129, 129),
                                            start=(starts and i == 0),
                                            stop=(stops and i == n - 1))
                                        i += 1
                                return last
                            return f
                        P.op("tensor", mkpv(),
                             reads=prhk + prlk + VA_ALL,
                             writes=[("psO", bhalf)])

                P.op("vector", lambda e: e.reciprocal(
                    invz[:].rearrange("p (j o) -> p j o", o=1),
                    ps_o[:].rearrange("p (j c) -> p j c", c=256)[:, :, 128:129]),
                    reads=[("psO", 0), ("psO", 1)], writes=["invz"],
                    small=True)

                def mkoev(e):
                    last = None
                    for j in range(4):
                        last = e.tensor_scalar(
                            out=pap(O_OT + j * 128, 128),
                            in0=ps_o[:, j * 256:j * 256 + 128],
                            scalar1=invz[:, j:j + 1], scalar2=None,
                            op0=ALU.mult)
                    return last
                P.op("vector", mkoev,
                     reads=["invz", ("psO", 0), ("psO", 1)],
                     writes=pk(O_OT, 512))
                bank, bkey = mmT[0]

                def mkotr(bank=bank):
                    def f(e):
                        last = None
                        for j in range(4):
                            last = e.matmul(
                                bank[:, j * 128:(j + 1) * 128],
                                pap(O_OT + j * 128, 128), ident[:],
                                is_transpose=True, start=(j == 0), stop=(j == 3))
                        return last
                    return f
                P.op("tensor", mkotr(), reads=pk(O_OT, 512) + ["ident"],
                     writes=[bkey])

                def mkoevac(t4=t4, bank=bank):
                    def f(e):
                        return e.activation(out=bap(B_OFH + t4 * 512, 512),
                                            in_=bank[:], func=AF.Copy)
                    return f
                P.op("scalar", mkoevac(), reads=[bkey],
                     writes=bk(B_OFH + t4 * 512, 512))

                def mkoevlo(t4=t4, bank=bank):
                    def f(e):
                        return e.tensor_sub(bap(B_OFL + t4 * 512, 512),
                                            bank[:], bap(B_OFH + t4 * 512, 512))
                    return f
                P.op("vector", mkoevlo(),
                     reads=[bkey] + bk(B_OFH + t4 * 512, 512),
                     writes=bk(B_OFL + t4 * 512, 512))
            OF_ALL = bk(B_OFH, 2048) + bk(B_OFL, 2048)
            wo_hi, wo_lo, wo_keys = get_w(h, 3)
            for tb in range(NTB):
                for ot in range(2):
                    bank, bkey = mmA[ot]

                    def mkmm(tb=tb, ot=ot, bank=bank, wo_hi=wo_hi,
                             wo_lo=wo_lo):
                        def f(e):
                            last = None
                            OPASS = ((B_OFH, wo_hi), (B_OFL, wo_hi),
                                     (B_OFH, wo_lo))
                            for i, (ob, wap) in enumerate(OPASS):
                                last = e.matmul(
                                    bank[:],
                                    bap(ob + tb * 128, 128),
                                    wap[:, ot * 512:(ot + 1) * 512],
                                    start=(i == 0), stop=(i == 2))
                            return last
                        return f
                    P.op("tensor", mkmm(), reads=OF_ALL + wo_keys,
                         writes=[bkey])

                    def mkadd(tb=tb, ot=ot, bank=bank):
                        def f(e):
                            return e.scalar_tensor_tensor(
                                out=rap(tb, ot * 512, (ot + 1) * 512),
                                in0=bank[:], scalar=1.0,
                                in1=rap(tb, ot * 512, (ot + 1) * 512),
                                op0=ALU.mult, op1=ALU.add)
                        return f
                    P.op("vector", mkadd(), reads=[bkey, rk(tb)],
                         writes=[rk(tb)])

    if stages >= 3:
        ffn_stage()

    P.op("scalar", lambda e: e.dma_start(
        out=out_t.rearrange("(tt p) f -> p tt f", p=128),
        in_=resid[:].rearrange("p (tt f) -> p tt f", tt=NTB)),
        reads=[rk(tb) for tb in range(NTB)], dma_sem="dout")

    per_eng, ctr = P.finalize()
    dout_total = ctr.get("dout", 0)

    with nc.Block() as block:
        def runner(eng_name):
            def run(eng):
                for fn, waits, inc_sem, inc_amt in per_eng[eng_name]:
                    for semname, val in waits:
                        eng.wait_ge(sems[semname], val)
                    inst = fn(eng)
                    if inc_sem is not None:
                        assert inst is not None
                        inst.then_inc(sems[inc_sem], inc_amt)
                if eng_name == "sync":
                    eng.wait_ge(sems["dout"], dout_total)
            return run

        block.sync(runner("sync"))
        block.tensor(runner("tensor"))
        block.vector(runner("vector"))
        block.scalar(runner("scalar"))
        block.gpsimd(runner("gpsimd"))

    es.close()
    return nc


_NC_CACHE = {}


def kernel(L0_attn_x, L0_attn_Wq, L0_attn_Wk, L0_attn_Wv, L0_attn_Wo,
           L0_ffn_weight_q, L0_ffn_weight_scale, L0_ffn_bias):
    x = np.ascontiguousarray(np.asarray(L0_attn_x, dtype=np.float32))
    wq = np.ascontiguousarray(np.asarray(L0_attn_Wq, dtype=np.float32))
    wk = np.ascontiguousarray(np.asarray(L0_attn_Wk, dtype=np.float32))
    wv = np.ascontiguousarray(np.asarray(L0_attn_Wv, dtype=np.float32))
    wo = np.ascontiguousarray(np.asarray(L0_attn_Wo, dtype=np.float32))
    fw = np.ascontiguousarray(np.asarray(L0_ffn_weight_q, dtype=np.int8))
    fs = np.ascontiguousarray(np.asarray(L0_ffn_weight_scale, dtype=np.float32))
    fb = np.ascontiguousarray(np.asarray(L0_ffn_bias, dtype=np.float32))

    if "nc" not in _NC_CACHE:
        _NC_CACHE["nc"] = build_nc(stages=3)
    nc = _NC_CACHE["nc"]
    in_maps = [{"x": x[b], "wq": wq, "wk": wk, "wv": wv, "wo": wo,
                "fw": fw, "fs": fs, "fb": fb} for b in range(8)]
    res = run_bass_kernel_spmd(nc, in_maps, core_ids=list(range(8)))
    return np.stack([res.results[b]["out"] for b in range(8)], axis=0)


# version 09
# speedup vs baseline: 1.2230x; 1.2230x over previous
import numpy as np
import concourse.bass as bass
import concourse.mybir as mybir
from concourse.bass_utils import run_bass_kernel_spmd
from contextlib import ExitStack

F32 = mybir.dt.float32
BF16 = mybir.dt.bfloat16
I16 = mybir.dt.int16
I8 = mybir.dt.int8
AF = mybir.ActivationFunctionType
ALU = mybir.AluOpType
AX = mybir.AxisListType

T = 2048
D = 1024
NH = 16
NTB = 16
NFC = 8
NT4 = 4

ENGINES = ("sync", "tensor", "vector", "scalar", "gpsimd")
FFN_TILES = NTB


class Op:
    __slots__ = ("idx", "eng", "fn", "reads", "writes", "sig", "sem",
                 "deps", "sig_val", "small")

    def __init__(self, idx, eng, fn, reads, writes, sig, sem, small):
        self.idx, self.eng, self.fn = idx, eng, fn
        self.reads, self.writes = tuple(reads), tuple(writes)
        self.sig, self.sem = sig, sem
        self.deps, self.sig_val = [], None
        self.small = small


class Prog:
    SERIAL = False

    def __init__(self):
        self.ops = []
        self.last_writer = {}
        self.readers = {}

    def op(self, eng, fn, reads=(), writes=(), dma_sem=None, small=False):
        idx = len(self.ops)
        o = Op(idx, eng, fn, reads, writes, 16 if dma_sem else 0,
               dma_sem or eng, small)
        deps = set()
        if Prog.SERIAL and idx > 0:
            deps.add(idx - 1)
        for b in o.reads:
            w = self.last_writer.get(b)
            if w is not None:
                deps.add(w)
        for b in o.writes:
            w = self.last_writer.get(b)
            if w is not None:
                deps.add(w)
            deps.update(self.readers.get(b, ()))
        deps.discard(idx)
        o.deps = sorted(deps)
        self.ops.append(o)
        for b in o.reads:
            self.readers.setdefault(b, []).append(idx)
        for b in o.writes:
            self.last_writer[b] = idx
            self.readers[b] = []
        return idx

    def finalize(self):
        ops = self.ops
        needed = set()
        for o in ops:
            for d in o.deps:
                if ops[d].eng != o.eng or ops[d].small:
                    needed.add(d)
        ctr = {}
        for o in ops:
            if o.sig == 16:
                ctr[o.sem] = ctr.get(o.sem, 0) + 16
                o.sig_val = ctr[o.sem]
            elif o.idx in needed:
                o.sig = 1
                ctr[o.sem] = ctr.get(o.sem, 0) + 1
                o.sig_val = ctr[o.sem]
        per_eng = {e: [] for e in ENGINES}
        waited = {e: {} for e in ENGINES}
        for o in ops:
            waits = {}
            for d in o.deps:
                od = ops[d]
                if od.eng == o.eng and not od.small:
                    continue
                sem = od.sem
                if waits.get(sem, 0) < od.sig_val:
                    waits[sem] = od.sig_val
            w2 = []
            for sem, val in waits.items():
                if waited[o.eng].get(sem, 0) < val:
                    waited[o.eng][sem] = val
                    w2.append((sem, val))
            per_eng[o.eng].append((o.fn, w2, o.sem if o.sig else None, o.sig))
        return per_eng, ctr


def _gran(off, n, tag):
    return [(tag, g) for g in range(off // 512, (off + n + 511) // 512)]


def build_nc(stages=3):
    nc = bass.Bass("TRN2", target_bir_lowering=False)
    x_in = nc.dram_tensor("x", [T, D], F32, kind="ExternalInput").ap()
    wq_in = nc.dram_tensor("wq", [D, 2048], F32, kind="ExternalInput").ap()
    wk_in = nc.dram_tensor("wk", [D, 2048], F32, kind="ExternalInput").ap()
    wv_in = nc.dram_tensor("wv", [D, 2048], F32, kind="ExternalInput").ap()
    wo_in = nc.dram_tensor("wo", [2048, D], F32, kind="ExternalInput").ap()
    fw_in = nc.dram_tensor("fw", [D, D], I8, kind="ExternalInput").ap()
    fs_in = nc.dram_tensor("fs", [D], F32, kind="ExternalInput").ap()
    fb_in = nc.dram_tensor("fb", [D], F32, kind="ExternalInput").ap()
    out_t = nc.dram_tensor("out", [T, D], F32, kind="ExternalOutput").ap()

    es = ExitStack()
    sb = lambda n, shp, dt: es.enter_context(nc.sbuf_tensor(n, shp, dt))

    resid = sb("resid", [128, NTB * D], F32)
    pool = sb("pool", [128, 9232], F32)
    bpool = sb("bpool", [128, 49152], BF16)
    i8p = sb("i8p", [128, 1024], I8)
    ci = sb("ci", [128, D], I16)
    stats = sb("stats", [128, 16], F32)
    invz = sb("invz", [128, 4], F32)
    onesq = sb("onesq", [128, 128], F32)
    ident = sb("ident", [128, 128], F32)
    onesb = sb("onesb", [128, 128], BF16)
    identb = sb("identb", [128, 128], BF16)

    O_ZG = (0, 1024)
    O_RD = (2048, 3072)
    O_CIF = (4096, 4096)
    O_HB, O_WS = 5120, 6144
    O_WSROW, O_FBROW = 0, 1024
    O_ST2 = 7168
    SUMS, SSQS, MEANS, SVEC, MXS, MNS, XSCS, IXSS, GS, XSHS = range(10)
    O_OT, N_OT = 0, 512
    O_VF, N_VF = 512, 2048
    O_WSTG = 5120
    O_ZT2, O_SQ2 = 3584, 2560
    O_PRF = (6144, 6656)
    O_VA, N_VA = 7168, 2064
    POOL_COLS = 9232

    def pap(off, n):
        return pool[:, off:off + n]

    def pk(off, n):
        return _gran(off, n, "P")

    B_WT = 0
    B_XQTM, B_XQFM = 8192, 9216
    B_Y2H, B_Y2L = 0, 16384
    B_WP = 32768
    B_FWBF = 8192
    B_KHH, B_KHL = 36864, 38912
    B_QHH, B_QHL = 40960, 43008
    B_OFH, B_OFL = 45056, 47104
    BPOOL_ELS = 49152

    def bap(off, n):
        return bpool[:, off:off + n]

    def bk(off, n):
        return _gran(off, n, "B")

    ps_a = es.enter_context(nc.psum_tensor("ps_a", [128, 512], F32))
    ps_b = es.enter_context(nc.psum_tensor("ps_b", [128, 512], F32))
    ps_c = es.enter_context(nc.psum_tensor("ps_c", [128, 512], F32))
    ps_d = es.enter_context(nc.psum_tensor("ps_d", [128, 512], F32))
    ps_o = es.enter_context(nc.psum_tensor("ps_o", [128, 1024], F32))
    ps_e = es.enter_context(nc.psum_tensor("ps_e", [128, 512], F32))
    ps_tb = es.enter_context(nc.psum_tensor("ps_tb", [128, 512], BF16))

    semnames = list(ENGINES) + ["din", "dinx", "dinw", "dinr1", "dinr2", "dout"]
    sems = {n: es.enter_context(nc.semaphore("s_" + n)) for n in semnames}

    P = Prog()
    mmA = [(ps_a, "psA"), (ps_b, "psB")]
    mmT = [(ps_c, "psC"), (ps_d, "psD")]

    def col(c):
        return stats[:, c:c + 1]

    C_MEAN, C_S, C_SUM, C_SSQ, C_VAR, C_SD, C_MSQ, C_AMAX, C_XSC, C_IXS, \
        C_XSH = range(11)

    def rap(tb, c0=0, c1=D):
        return resid[:, tb * D + c0: tb * D + c1]

    def rk(tb):
        return ("R", tb)

    P.op("sync", lambda e: e.dma_start(
        out=resid[:].rearrange("p (tt f) -> p tt f", tt=NTB),
        in_=x_in.rearrange("(tt p) f -> p tt f", p=128)),
        writes=[rk(tb) for tb in range(NTB)], dma_sem="dinx")

    P.op("vector", lambda e: e.memset(onesq[:], 1.0), writes=["onesq"])
    P.op("vector", lambda e: e.memset(onesb[:], 1.0), writes=["onesb"])
    P.op("gpsimd", lambda e: e.affine_select(
        out=ident[:], in_=onesq[:], compare_op=ALU.is_equal,
        pattern=[[-1, 128]], base=0, channel_multiplier=1, fill=0.0),
        reads=["onesq"], writes=["ident"])
    P.op("gpsimd", lambda e: e.affine_select(
        out=identb[:], in_=onesb[:], compare_op=ALU.is_equal,
        pattern=[[-1, 128]], base=0, channel_multiplier=1, fill=0.0),
        reads=["onesb"], writes=["identb"])

    def build_wt2():
        for oc in range(NFC):
            def mkd(oc=oc):
                def f(e):
                    return e.dma_start(out=i8p[:],
                                       in_=fw_in[oc * 128:(oc + 1) * 128, :])
                return f
            P.op("sync", mkd(), writes=["i8p"], dma_sem="din")
            P.op("vector", lambda e: e.tensor_copy(bap(B_FWBF, 1024), i8p[:]),
                 reads=["i8p"], writes=bk(B_FWBF, 1024))
            for g in range(2):
                def mktr(g=g):
                    def f(e):
                        last = None
                        for j in range(4):
                            icc = g * 4 + j
                            last = e.matmul(
                                ps_tb[:, j * 128:(j + 1) * 128],
                                bap(B_FWBF + icc * 128, 128), identb[:],
                                is_transpose=True, start=(j == 0), stop=(j == 3))
                        return last
                    return f
                P.op("tensor", mktr(), reads=bk(B_FWBF, 1024) + ["identb"],
                     writes=["psTB"])

                def mkev(oc=oc, g=g):
                    def f(e):
                        return e.activation(
                            out=bpool[:, B_WT:B_WT + 8192]
                            .rearrange("p (ic o) -> p ic o", ic=NFC)
                            [:, g * 4:(g + 1) * 4, oc * 128:(oc + 1) * 128],
                            in_=ps_tb[:].rearrange("p (a b) -> p a b", a=4),
                            func=AF.Copy)
                    return f
                wks = []
                for icc in range(g * 4, g * 4 + 4):
                    wks += _gran(B_WT + icc * 1024 + oc * 128, 128, "B")
                P.op("scalar", mkev(), reads=["psTB"], writes=wks)

    WTB_ALL = bk(B_WT, 8192)
    def ln_stats(src_ap, src_keys, o_sq):
        sq_ap = pap(o_sq, D)
        sq_k = pk(o_sq, D)
        P.op("vector", lambda e: e.tensor_reduce(
            out=col(C_SUM), in_=src_ap, op=ALU.add, axis=AX.X),
            reads=src_keys, writes=[("st", C_SUM)], small=True)
        P.op("scalar", lambda e: e.activation(out=sq_ap, in_=src_ap,
                                              func=AF.Square),
            reads=src_keys, writes=sq_k)
        P.op("vector", lambda e: e.tensor_reduce(
            out=col(C_SSQ), in_=sq_ap, op=ALU.add, axis=AX.X),
            reads=sq_k, writes=[("st", C_SSQ)], small=True)
        P.op("vector", lambda e: e.tensor_scalar(
            out=col(C_MEAN), in0=col(C_SUM), scalar1=1.0 / D, scalar2=None,
            op0=ALU.mult), reads=[("st", C_SUM)], writes=[("st", C_MEAN)],
            small=True)
        P.op("vector", lambda e: e.tensor_scalar(
            out=col(C_MSQ), in0=col(C_MEAN), scalar1=col(C_MEAN),
            scalar2=-1e-5, op0=ALU.mult, op1=ALU.add),
            reads=[("st", C_MEAN)], writes=[("st", C_MSQ)], small=True)
        P.op("vector", lambda e: e.scalar_tensor_tensor(
            out=col(C_VAR), in0=col(C_SSQ), scalar=1.0 / D, in1=col(C_MSQ),
            op0=ALU.mult, op1=ALU.subtract),
            reads=[("st", C_SSQ), ("st", C_MSQ)], writes=[("st", C_VAR)],
            small=True)
        P.op("scalar", lambda e: e.activation(
            out=col(C_SD), in_=col(C_VAR), func=AF.Sqrt),
            reads=[("st", C_VAR)], writes=[("st", C_SD)], small=True)
        P.op("vector", lambda e: e.reciprocal(col(C_S), col(C_SD)),
             reads=[("st", C_SD)], writes=[("st", C_S)], small=True)

    def bcast_rows(src_off, dst_off, scale):
        for half in range(2):
            def mkmm(half=half):
                def f(e):
                    last = None
                    for j in range(4):
                        r = half * 4 + j
                        last = e.matmul(ps_e[:, j * 128:(j + 1) * 128],
                                        onesq[0:1, :],
                                        pool[0:1, src_off + r * 128:
                                             src_off + (r + 1) * 128],
                                        start=(j == 0), stop=(j == 3))
                    return last
                return f
            P.op("tensor", mkmm(), reads=pk(src_off, D) + ["onesq"],
                 writes=["psE"])

            def mkev(half=half):
                def f(e):
                    return e.activation(
                        out=pap(dst_off + half * 512, 512), in_=ps_e[:],
                        func=AF.Copy, scale=scale)
                return f
            P.op("scalar", mkev(), reads=["psE"],
                 writes=pk(dst_off + half * 512, 512))

    ffn_calls = [0]

    def st2(c, t=None):
        base = O_ST2 + c * 16
        if t is None:
            return pool[:, base:base + 16]
        return pool[:, base + t:base + t + 1]

    STK = pk(O_ST2, 11 * 16)
    TMPS = 10

    def ffn_stage():
        ffn_calls[0] += 1
        this_call = ffn_calls[0]
        P.op("sync", lambda e: e.dma_start(
            out=pool[0:1, O_WSROW:O_WSROW + D],
            in_=fs_in.rearrange("(o d) -> o d", o=1)),
            writes=pk(O_WSROW, D), dma_sem="dinr1")
        P.op("sync", lambda e: e.dma_start(
            out=pool[0:1, O_FBROW:O_FBROW + D],
            in_=fb_in.rearrange("(o d) -> o d", o=1)),
            writes=pk(O_FBROW, D), dma_sem="dinr2")
        bcast_rows(O_WSROW, O_WS, 1.0)
        bcast_rows(O_FBROW, O_HB, 0.5)
        build_wt2()
        hb, hbk = pap(O_HB, D), pk(O_HB, D)
        wsk = pk(O_WS, D)

        for tb in range(NTB):
            srcap = rap(tb)
            zgd, zgdk = pap(O_ZG[tb % 2], D), pk(O_ZG[tb % 2], D)
            P.op("vector", lambda e, s=srcap, t=tb, zgd=zgd: e.tensor_scalar(
                out=zgd, in0=s, scalar1=1.0, scalar2=None, op0=ALU.mult,
                op1=ALU.add, accum_out=st2(SUMS, t)),
                reads=[rk(tb)], writes=zgdk + STK, small=True)
            P.op("scalar", lambda e, s=srcap, zgd=zgd: e.activation(
                out=zgd, in_=s, func=AF.Square),
                reads=[rk(tb)], writes=zgdk)
            P.op("vector", lambda e, zgd=zgd, t=tb: e.tensor_scalar(
                out=zgd, in0=zgd, scalar1=1.0, scalar2=None,
                op0=ALU.mult, op1=ALU.add, accum_out=st2(SSQS, t)),
                reads=zgdk, writes=zgdk + STK, small=True)

        P.op("vector", lambda e: e.tensor_scalar(
            out=st2(MEANS), in0=st2(SUMS), scalar1=1.0 / D, scalar2=None,
            op0=ALU.mult), reads=STK, writes=STK, small=True)
        P.op("vector", lambda e: e.tensor_mul(
            st2(TMPS), st2(MEANS), st2(MEANS)), reads=STK, writes=STK,
            small=True)
        P.op("vector", lambda e: e.tensor_scalar(
            out=st2(TMPS), in0=st2(TMPS), scalar1=-1e-5, scalar2=None,
            op0=ALU.add), reads=STK, writes=STK, small=True)
        P.op("vector", lambda e: e.scalar_tensor_tensor(
            out=st2(TMPS), in0=st2(SSQS), scalar=1.0 / D, in1=st2(TMPS),
            op0=ALU.mult, op1=ALU.subtract), reads=STK, writes=STK, small=True)
        P.op("scalar", lambda e: e.activation(
            out=st2(TMPS), in_=st2(TMPS), func=AF.Sqrt),
            reads=STK, writes=STK, small=True)
        P.op("vector", lambda e: e.reciprocal(st2(SVEC), st2(TMPS)),
             reads=STK, writes=STK, small=True)

        for tb in range(NTB):
            srcap = rap(tb)
            zgd, zgdk = pap(O_ZG[tb % 2], D), pk(O_ZG[tb % 2], D)
            P.op("vector", lambda e, s=srcap, t=tb, zgd=zgd: e.tensor_scalar(
                out=zgd, in0=s, scalar1=st2(MEANS, t), scalar2=None,
                op0=ALU.subtract, op1=ALU.max, accum_out=st2(MXS, t)),
                reads=[rk(tb)] + STK, writes=zgdk + STK, small=True)
            P.op("vector", lambda e, s=srcap, t=tb, zgd=zgd: e.tensor_scalar(
                out=zgd, in0=s, scalar1=st2(MEANS, t), scalar2=None,
                op0=ALU.subtract, op1=ALU.min, accum_out=st2(MNS, t)),
                reads=[rk(tb)] + STK, writes=zgdk + STK, small=True)

        P.op("vector", lambda e: e.tensor_scalar(
            out=st2(TMPS), in0=st2(MNS), scalar1=-1.0, scalar2=None,
            op0=ALU.mult), reads=STK, writes=STK, small=True)
        P.op("vector", lambda e: e.tensor_max(
            st2(TMPS), st2(TMPS), st2(MXS)), reads=STK, writes=STK, small=True)
        P.op("vector", lambda e: e.tensor_mul(
            st2(TMPS), st2(TMPS), st2(SVEC)), reads=STK, writes=STK, small=True)
        P.op("vector", lambda e: e.tensor_scalar(
            out=st2(XSCS), in0=st2(TMPS), scalar1=1e-6, scalar2=1.0 / 127.0,
            op0=ALU.max, op1=ALU.mult), reads=STK, writes=STK, small=True)
        P.op("vector", lambda e: e.reciprocal(st2(IXSS), st2(XSCS)),
             reads=STK, writes=STK, small=True)
        P.op("vector", lambda e: e.tensor_mul(
            st2(GS), st2(SVEC), st2(IXSS)), reads=STK, writes=STK, small=True)
        P.op("vector", lambda e: e.tensor_scalar(
            out=st2(XSHS), in0=st2(XSCS), scalar1=0.5, scalar2=None,
            op0=ALU.mult), reads=STK, writes=STK, small=True)

        for tb in range(NTB):
            srcap = rap(tb)
            p_ = tb % 2
            zg, zgk = pap(O_ZG[p_], D), pk(O_ZG[p_], D)
            rd, rdk = pap(O_RD[p_], D), pk(O_RD[p_], D)
            cf, cfk = pap(O_CIF[p_], D), pk(O_CIF[p_], D)
            ci_ap = ci[:, 0:D]
            cik = ("ci", 0)
            P.op("vector", lambda e, s=srcap, t=tb, zg=zg: e.tensor_scalar(
                out=zg, in0=s, scalar1=st2(MEANS, t), scalar2=st2(GS, t),
                op0=ALU.subtract, op1=ALU.mult),
                reads=[rk(tb)] + STK, writes=zgk)
            P.op("scalar", lambda e, zg=zg, ci_ap=ci_ap: e.activation(
                out=ci_ap, in_=zg, func=AF.Copy, bias=128.5),
                reads=zgk, writes=[cik])
            P.op("scalar", lambda e, cf=cf, ci_ap=ci_ap: e.activation(
                out=cf, in_=ci_ap, func=AF.Copy), reads=[cik], writes=cfk)
            P.op("vector", lambda e, zg=zg, rd=rd, cf=cf: e.scalar_tensor_tensor(
                out=rd, in0=zg, scalar=128.5, in1=cf,
                op0=ALU.add, op1=ALU.subtract), reads=zgk + cfk, writes=rdk)
            P.op("vector", lambda e, rd=rd: e.tensor_scalar(
                out=rd, in0=rd, scalar1=0.0, scalar2=None, op0=ALU.is_ge),
                reads=rdk, writes=rdk)
            P.op("vector", lambda e, rd=rd, cf=cf, zg=zg: e.scalar_tensor_tensor(
                out=zg, in0=rd, scalar=1.0, in1=cf, op0=ALU.mult, op1=ALU.add),
                reads=rdk + cfk, writes=zgk)
            P.op("scalar", lambda e, zg=zg: e.activation(
                out=bap(B_XQTM, 1024), in_=zg, func=AF.Copy, bias=-129.0),
                reads=zgk, writes=bk(B_XQTM, 1024))
            P.op("gpsimd", lambda e, s=srcap: e.tensor_add(s, s, hb),
                 reads=[rk(tb)] + hbk, writes=[rk(tb)])
            for g in range(2):
                def mktr(g=g):
                    def f(e):
                        last = None
                        for j in range(4):
                            icc = g * 4 + j
                            last = e.matmul(
                                ps_tb[:, j * 128:(j + 1) * 128],
                                bap(B_XQTM + icc * 128, 128), identb[:],
                                is_transpose=True, start=(j == 0), stop=(j == 3))
                        return last
                    return f
                P.op("tensor", mktr(), reads=bk(B_XQTM, 1024) + ["identb"],
                     writes=["psTB"])
                P.op("scalar", lambda e, g=g: e.activation(
                    out=bap(B_XQFM + g * 512, 512), in_=ps_tb[:],
                    func=AF.Copy), reads=["psTB"],
                    writes=bk(B_XQFM + g * 512, 512))
            for oh in range(2):
                bank, bkey = mmA[oh]

                def mkmm(oh=oh, bank=bank):
                    def f(e):
                        last = None
                        for icc in range(NFC):
                            last = e.matmul(
                                bank[:],
                                bap(B_XQFM + icc * 128, 128),
                                bpool[:, B_WT + icc * D + oh * 512:
                                      B_WT + icc * D + (oh + 1) * 512],
                                start=(icc == 0), stop=(icc == NFC - 1))
                        return last
                    return f
                P.op("tensor", mkmm(),
                     reads=WTB_ALL + bk(B_XQFM, 1024), writes=[bkey])
                uo = O_RD[tb % 2] + oh * 512
                P.op("vector", lambda e, oh=oh, bank=bank, uo=uo: e.tensor_mul(
                    pap(uo, 512), bank[:],
                    pool[:, O_WS + oh * 512:O_WS + (oh + 1) * 512]),
                    reads=[bkey] + wsk, writes=pk(uo, 512))
                P.op("vector", lambda e, oh=oh, s=srcap, t=tb, uo=uo:
                     e.scalar_tensor_tensor(
                         out=s[:, oh * 512:(oh + 1) * 512], in0=pap(uo, 512),
                         scalar=st2(XSHS, t), in1=s[:, oh * 512:(oh + 1) * 512],
                         op0=ALU.mult, op1=ALU.add),
                     reads=pk(uo, 512) + STK + [rk(tb)], writes=[rk(tb)])

    if stages >= 1:
        ffn_stage()

    if stages >= 2:
        for tb in range(NTB):
            src = rap(tb)
            ln_stats(src, [rk(tb)], O_SQ2)
            z2, z2k = pap(O_ZT2, D), pk(O_ZT2, D)
            P.op("vector", lambda e, s=src: e.tensor_scalar(
                out=z2, in0=s, scalar1=col(C_MEAN), scalar2=col(C_S),
                op0=ALU.subtract, op1=ALU.mult),
                reads=[rk(tb), ("st", C_MEAN), ("st", C_S)], writes=z2k)
            for g in range(2):
                bank, bkey = mmT[g]

                def mktr(g=g, z2=z2, bank=bank):
                    def f(e):
                        last = None
                        for j in range(4):
                            fc = g * 4 + j
                            last = e.matmul(
                                bank[:, j * 128:(j + 1) * 128],
                                z2[:, fc * 128:(fc + 1) * 128], ident[:],
                                is_transpose=True, start=(j == 0), stop=(j == 3))
                        return last
                    return f
                P.op("tensor", mktr(), reads=z2k + ["ident"], writes=[bkey])
                def mkev(g=g, tb=tb, bank=bank):
                    def f(e):
                        return e.activation(
                            out=bpool[:, B_Y2H:B_Y2H + 16384]
                            .rearrange("p (fc t) -> p fc t", fc=NFC)
                            [:, g * 4:(g + 1) * 4, tb * 128:(tb + 1) * 128],
                            in_=bank[:].rearrange("p (a b) -> p a b", a=4),
                            func=AF.Copy)
                    return f
                y2ks = []
                for fc in range(g * 4, g * 4 + 4):
                    y2ks += _gran(B_Y2H + fc * T + tb * 128, 128, "B")
                P.op("scalar", mkev(), reads=[bkey], writes=y2ks)

                def mklo(g=g, tb=tb, bank=bank):
                    def f(e):
                        return e.tensor_sub(
                            bpool[:, B_Y2L:B_Y2L + 16384]
                            .rearrange("p (fc t) -> p fc t", fc=NFC)
                            [:, g * 4:(g + 1) * 4, tb * 128:(tb + 1) * 128],
                            bank[:].rearrange("p (a b) -> p a b", a=4),
                            bpool[:, B_Y2H:B_Y2H + 16384]
                            .rearrange("p (fc t) -> p fc t", fc=NFC)
                            [:, g * 4:(g + 1) * 4, tb * 128:(tb + 1) * 128])
                    return f
                y2kl = []
                for fc in range(g * 4, g * 4 + 4):
                    y2kl += _gran(B_Y2L + fc * T + tb * 128, 128, "B")
                P.op("vector", mklo(), reads=[bkey] + y2ks, writes=y2kl)
        Y2_ALL = bk(B_Y2H, 16384) + bk(B_Y2L, 16384)

        wdma_ctr = [0]

        def wdma(h, p):
            buf = wdma_ctr[0] % 2
            wdma_ctr[0] += 1
            hi_off = B_WP + buf * 2048
            lo_off = hi_off + 1024
            if p < 3:
                w_in = (wq_in, wk_in, wv_in)[p]

                def f32(e, w_in=w_in, h=h):
                    return e.dma_start(
                        out=pap(O_WSTG, 1024).rearrange("p (fc o) -> p fc o", fc=NFC),
                        in_=w_in.rearrange("(fc p) o -> p fc o", p=128)
                        [:, :, h * 128:(h + 1) * 128])
            else:
                def f32(e, h=h):
                    return e.dma_start(out=pap(O_WSTG, 1024),
                                       in_=wo_in[h * 128:(h + 1) * 128, :])
            P.op("sync", f32, writes=pk(O_WSTG, 1024), dma_sem="dinw")
            P.op("scalar", lambda e, hi_off=hi_off: e.activation(
                out=bap(hi_off, 1024), in_=pap(O_WSTG, 1024), func=AF.Copy),
                reads=pk(O_WSTG, 1024), writes=bk(hi_off, 1024))
            P.op("vector", lambda e, hi_off=hi_off, lo_off=lo_off: e.tensor_sub(
                bap(lo_off, 1024), pap(O_WSTG, 1024), bap(hi_off, 1024)),
                reads=pk(O_WSTG, 1024) + bk(hi_off, 1024),
                writes=bk(lo_off, 1024))
            return (bap(hi_off, 1024), bap(lo_off, 1024),
                    bk(hi_off, 1024) + bk(lo_off, 1024))

        pending = {}
        pending[(0, 0)] = wdma(0, 0)
        pending[(0, 1)] = wdma(0, 1)

        def get_w(h, p):
            if (h, p) not in pending:
                pending[(h, p)] = wdma(h, p)
            np_ = (h, p + 1) if p < 3 else (h + 1, 0)
            if np_[0] < NH and np_ not in pending:
                pending[np_] = wdma(*np_)
            return pending.pop((h, p))

        for h in range(NH):
            def proj(h, p, dst_off, pair=None):
                w_hi, w_lo, w_keys = get_w(h, p)
                for t4 in range(NT4):
                    bank, bkey = mmA[t4 % 2]

                    def mkmm(t4=t4, bank=bank, w_hi=w_hi, w_lo=w_lo):
                        def f(e):
                            last = None
                            n = 3 * NFC
                            i = 0
                            for fc in range(NFC):
                                for (wap, yoff) in ((w_hi, B_Y2H), (w_lo, B_Y2H),
                                                    (w_hi, B_Y2L)):
                                    last = e.matmul(
                                        bank[:],
                                        wap[:, fc * 128:(fc + 1) * 128],
                                        bpool[:, yoff + fc * T + t4 * 512:
                                              yoff + fc * T + (t4 + 1) * 512],
                                        start=(i == 0), stop=(i == n - 1))
                                    i += 1
                            return last
                        return f
                    P.op("tensor", mkmm(), reads=Y2_ALL + w_keys, writes=[bkey])
                    if pair is None:
                        def mkev(t4=t4, bank=bank, dst_off=dst_off):
                            def f(e):
                                return e.activation(
                                    out=pap(dst_off + t4 * 512, 512),
                                    in_=bank[:], func=AF.Copy)
                            return f
                        P.op("scalar", mkev(), reads=[bkey],
                             writes=pk(dst_off + t4 * 512, 512))
                    else:
                        bh, bl = pair
                        def mkhi(t4=t4, bank=bank, bh=bh):
                            def f(e):
                                return e.activation(
                                    out=bap(bh + t4 * 512, 512),
                                    in_=bank[:], func=AF.Copy)
                            return f
                        P.op("scalar", mkhi(), reads=[bkey],
                             writes=bk(bh + t4 * 512, 512))
                        def mklo(t4=t4, bank=bank, bh=bh, bl=bl):
                            def f(e):
                                return e.tensor_sub(
                                    bap(bl + t4 * 512, 512), bank[:],
                                    bap(bh + t4 * 512, 512))
                            return f
                        P.op("vector", mklo(),
                             reads=[bkey] + bk(bh + t4 * 512, 512),
                             writes=bk(bl + t4 * 512, 512))

            proj(h, 0, None, pair=(B_QHH, B_QHL))
            proj(h, 1, None, pair=(B_KHH, B_KHL))
            proj(h, 2, O_VF, None)
            QH_ALL = bk(B_QHH, 2048) + bk(B_QHL, 2048)
            KH_ALL = bk(B_KHH, 2048) + bk(B_KHL, 2048)
            VF_ALL = pk(O_VF, N_VF)

            for g in range(4):
                bank, bkey = mmT[0]

                def mktr(g=g, bank=bank):
                    def f(e):
                        last = None
                        for j in range(4):
                            kb = g * 4 + j
                            last = e.matmul(
                                bank[:, j * 128:(j + 1) * 128],
                                pap(O_VF + kb * 128, 128), ident[:],
                                is_transpose=True, start=(j == 0), stop=(j == 3))
                        return last
                    return f
                P.op("tensor", mktr(), reads=VF_ALL + ["ident"],
                     writes=[bkey])

                def mkev(g=g, bank=bank):
                    def f(e):
                        return e.activation(
                            out=pap(O_VA, N_VA)
                            .rearrange("p (b c) -> p b c", c=129)
                            [:, g * 4:(g + 1) * 4, 0:128],
                            in_=bank[:].rearrange("p (a b) -> p a b", a=4),
                            func=AF.Copy)
                    return f
                P.op("scalar", mkev(), reads=[bkey],
                     writes=pk(O_VA + g * 4 * 129, 4 * 129))
            P.op("vector", lambda e: e.memset(
                pap(O_VA, N_VA).rearrange("p (b c) -> p b c", c=129)
                [:, :, 128:129], 1.0), writes=pk(O_VA, N_VA))
            VA_ALL = pk(O_VA, N_VA)

            for t4 in range(NT4):
                n_off = 4 * t4
                nkc = n_off + 4
                bank_first = {0: True, 1: True}
                bank_total = {0: 0, 1: 0}
                for kc in range(nkc):
                    jmin = 0 if kc < n_off else kc - n_off
                    for j in range(jmin, 4):
                        bank_total[j // 2] += 1
                bank_done = {0: 0, 1: 0}
                for kc in range(nkc):
                    diag = kc >= n_off
                    bank, bkey = mmA[kc % 2]
                    pslot = kc % 2
                    p_ap = pap(O_PRF[pslot], 512)
                    pkey = ("P", O_PRF[pslot] // 512)
                    jmin = 0 if not diag else kc - n_off

                    PASSES = ((B_KHH, B_QHH), (B_KHL, B_QHH), (B_KHH, B_QHL))
                    if not diag:
                        def mksc(kc=kc, bank=bank, t4=t4):
                            def f(e):
                                last = None
                                for i, (kb, qb) in enumerate(PASSES):
                                    last = e.matmul(
                                        bank[:],
                                        bap(kb + kc * 128, 128),
                                        bap(qb + t4 * 512, 512),
                                        start=(i == 0), stop=(i == 2))
                                return last
                            return f
                    else:
                        def mksc(kc=kc, bank=bank, t4=t4, jmin=jmin):
                            def f(e):
                                last = None
                                n = (4 - jmin) * 3
                                i = 0
                                for j in range(jmin, 4):
                                    for (kb, qb) in PASSES:
                                        last = e.matmul(
                                            bank[:, j * 128:(j + 1) * 128],
                                            bap(kb + kc * 128, 128),
                                            bap(qb + t4 * 512 + j * 128, 128),
                                            start=(i == 0), stop=(i == n - 1))
                                        i += 1
                                return last
                            return f
                    P.op("tensor", mksc(), reads=QH_ALL + KH_ALL,
                         writes=[bkey])

                    def mkexp(bank=bank, p_ap=p_ap, jmin=jmin):
                        def f(e):
                            return e.activation(
                                out=p_ap[:, jmin * 128:512],
                                in_=bank[:, jmin * 128:512],
                                func=AF.Exp, scale=0.125)
                        return f
                    P.op("scalar", mkexp(), reads=[bkey], writes=[pkey])
                    if diag:
                        def mkmask(p_ap=p_ap, jmin=jmin):
                            def f(e):
                                return e.affine_select(
                                    out=p_ap[:, jmin * 128:(jmin + 1) * 128],
                                    in_=p_ap[:, jmin * 128:(jmin + 1) * 128],
                                    compare_op=ALU.is_ge, pattern=[[1, 128]],
                                    base=0, channel_multiplier=-1, fill=0.0)
                            return f
                        P.op("gpsimd", mkmask(), reads=[pkey], writes=[pkey])

                    for bhalf in (0, 1):
                        js = [j for j in range(jmin, 4) if j // 2 == bhalf]
                        if not js:
                            continue
                        starts = bank_first[bhalf]
                        bank_first[bhalf] = False
                        bank_done[bhalf] += len(js)
                        stops = bank_done[bhalf] == bank_total[bhalf]

                        def mkpv(kc=kc, js=tuple(js), p_ap=p_ap,
                                 starts=starts, stops=stops):
                            def f(e):
                                last = None
                                for i, j in enumerate(js):
                                    last = e.matmul(
                                        ps_o[:, j * 256:j * 256 + 129],
                                        p_ap[:, j * 128:(j + 1) * 128],
                                        pap(O_VA + kc * 129, 129),
                                        start=(starts and i == 0),
                                        stop=(stops and i == len(js) - 1))
                                return last
                            return f
                        P.op("tensor", mkpv(), reads=[pkey] + VA_ALL,
                             writes=[("psO", bhalf)])

                P.op("vector", lambda e: e.reciprocal(
                    invz[:].rearrange("p (j o) -> p j o", o=1),
                    ps_o[:].rearrange("p (j c) -> p j c", c=256)[:, :, 128:129]),
                    reads=[("psO", 0), ("psO", 1)], writes=["invz"],
                    small=True)

                def mkoev(e):
                    last = None
                    for j in range(4):
                        last = e.tensor_scalar(
                            out=pap(O_OT + j * 128, 128),
                            in0=ps_o[:, j * 256:j * 256 + 128],
                            scalar1=invz[:, j:j + 1], scalar2=None,
                            op0=ALU.mult)
                    return last
                P.op("vector", mkoev,
                     reads=["invz", ("psO", 0), ("psO", 1)],
                     writes=pk(O_OT, 512))
                bank, bkey = mmT[0]

                def mkotr(bank=bank):
                    def f(e):
                        last = None
                        for j in range(4):
                            last = e.matmul(
                                bank[:, j * 128:(j + 1) * 128],
                                pap(O_OT + j * 128, 128), ident[:],
                                is_transpose=True, start=(j == 0), stop=(j == 3))
                        return last
                    return f
                P.op("tensor", mkotr(), reads=pk(O_OT, 512) + ["ident"],
                     writes=[bkey])

                def mkoevac(t4=t4, bank=bank):
                    def f(e):
                        return e.activation(out=bap(B_OFH + t4 * 512, 512),
                                            in_=bank[:], func=AF.Copy)
                    return f
                P.op("scalar", mkoevac(), reads=[bkey],
                     writes=bk(B_OFH + t4 * 512, 512))

                def mkoevlo(t4=t4, bank=bank):
                    def f(e):
                        return e.tensor_sub(bap(B_OFL + t4 * 512, 512),
                                            bank[:], bap(B_OFH + t4 * 512, 512))
                    return f
                P.op("vector", mkoevlo(),
                     reads=[bkey] + bk(B_OFH + t4 * 512, 512),
                     writes=bk(B_OFL + t4 * 512, 512))
            OF_ALL = bk(B_OFH, 2048) + bk(B_OFL, 2048)
            wo_hi, wo_lo, wo_keys = get_w(h, 3)
            for tb in range(NTB):
                for ot in range(2):
                    bank, bkey = mmA[ot]

                    def mkmm(tb=tb, ot=ot, bank=bank, wo_hi=wo_hi,
                             wo_lo=wo_lo):
                        def f(e):
                            last = None
                            OPASS = ((B_OFH, wo_hi), (B_OFL, wo_hi),
                                     (B_OFH, wo_lo))
                            for i, (ob, wap) in enumerate(OPASS):
                                last = e.matmul(
                                    bank[:],
                                    bap(ob + tb * 128, 128),
                                    wap[:, ot * 512:(ot + 1) * 512],
                                    start=(i == 0), stop=(i == 2))
                            return last
                        return f
                    P.op("tensor", mkmm(), reads=OF_ALL + wo_keys,
                         writes=[bkey])

                    def mkadd(tb=tb, ot=ot, bank=bank):
                        def f(e):
                            return e.scalar_tensor_tensor(
                                out=rap(tb, ot * 512, (ot + 1) * 512),
                                in0=bank[:], scalar=1.0,
                                in1=rap(tb, ot * 512, (ot + 1) * 512),
                                op0=ALU.mult, op1=ALU.add)
                        return f
                    P.op("vector", mkadd(), reads=[bkey, rk(tb)],
                         writes=[rk(tb)])

    if stages >= 3:
        ffn_stage()

    P.op("scalar", lambda e: e.dma_start(
        out=out_t.rearrange("(tt p) f -> p tt f", p=128),
        in_=resid[:].rearrange("p (tt f) -> p tt f", tt=NTB)),
        reads=[rk(tb) for tb in range(NTB)], dma_sem="dout")

    per_eng, ctr = P.finalize()
    dout_total = ctr.get("dout", 0)

    with nc.Block() as block:
        def runner(eng_name):
            def run(eng):
                for fn, waits, inc_sem, inc_amt in per_eng[eng_name]:
                    for semname, val in waits:
                        eng.wait_ge(sems[semname], val)
                    inst = fn(eng)
                    if inc_sem is not None:
                        assert inst is not None
                        inst.then_inc(sems[inc_sem], inc_amt)
                if eng_name == "sync":
                    eng.wait_ge(sems["dout"], dout_total)
            return run

        block.sync(runner("sync"))
        block.tensor(runner("tensor"))
        block.vector(runner("vector"))
        block.scalar(runner("scalar"))
        block.gpsimd(runner("gpsimd"))

    es.close()
    return nc


_NC_CACHE = {}


def kernel(L0_attn_x, L0_attn_Wq, L0_attn_Wk, L0_attn_Wv, L0_attn_Wo,
           L0_ffn_weight_q, L0_ffn_weight_scale, L0_ffn_bias):
    x = np.ascontiguousarray(np.asarray(L0_attn_x, dtype=np.float32))
    wq = np.ascontiguousarray(np.asarray(L0_attn_Wq, dtype=np.float32))
    wk = np.ascontiguousarray(np.asarray(L0_attn_Wk, dtype=np.float32))
    wv = np.ascontiguousarray(np.asarray(L0_attn_Wv, dtype=np.float32))
    wo = np.ascontiguousarray(np.asarray(L0_attn_Wo, dtype=np.float32))
    fw = np.ascontiguousarray(np.asarray(L0_ffn_weight_q, dtype=np.int8))
    fs = np.ascontiguousarray(np.asarray(L0_ffn_weight_scale, dtype=np.float32))
    fb = np.ascontiguousarray(np.asarray(L0_ffn_bias, dtype=np.float32))

    if "nc" not in _NC_CACHE:
        _NC_CACHE["nc"] = build_nc(stages=3)
    nc = _NC_CACHE["nc"]
    in_maps = [{"x": x[b], "wq": wq, "wk": wk, "wv": wv, "wo": wo,
                "fw": fw, "fs": fs, "fb": fb} for b in range(8)]
    res = run_bass_kernel_spmd(nc, in_maps, core_ids=list(range(8)))
    return np.stack([res.results[b]["out"] for b in range(8)], axis=0)
